# version 02 (final)
# baseline (speedup 1.0000x reference)
import math
import numpy as np

import concourse.bass as bass
import concourse.mybir as mybir
from concourse.bass_utils import run_bass_kernel_spmd

F32 = mybir.dt.float32
BF = mybir.dt.bfloat16
AF = mybir.ActivationFunctionType
ALU = mybir.AluOpType
AX = mybir.AxisListType

D = 1024
T = 2048
BPC = 2
NT = T // 128
NK = D // 128


class Em:

    def __init__(self, nc, sem):
        self.nc = nc
        self.sem = sem
        self.v = 0

    def op(self, eng, name, *a, **k):
        eng.wait_ge(self.sem, self.v)
        getattr(eng, name)(*a, **k).then_inc(self.sem, 1)
        self.v += 1

    def dma(self, out, in_, **k):
        e = self.nc.sync
        e.wait_ge(self.sem, self.v)
        e.dma_start(out=out, in_=in_, **k).then_inc(self.sem, 16)
        self.v += 16

    def mm(self, out, lhsT, rhs, start, stop, chain=False, sgc=False):
        e = self.nc.tensor
        if not chain:
            e.wait_ge(self.sem, self.v)
        e.matmul(out, lhsT, rhs, start=start, stop=stop,
                 skip_group_check=sgc).then_inc(self.sem, 1)
        self.v += 1

    def mm_acc(self, out, pairs):
        n = len(pairs)
        for i, (l, r) in enumerate(pairs):
            self.mm(out, l, r, start=(i == 0), stop=(i == n - 1), chain=(i > 0))

    def trans(self, out_psum, in_, eye):
        e = self.nc.tensor
        e.wait_ge(self.sem, self.v)
        e.transpose(out_psum, in_, eye).then_inc(self.sem, 1)
        self.v += 1

    def done(self):
        self.nc.sync.wait_ge(self.sem, self.v)


class Pool:
    def __init__(self, ap, n):
        self.ap = ap
        self.fl = [(0, n)]

    def alloc(self, w):
        w = (w + 127) & ~127
        for i, (o, sz) in enumerate(self.fl):
            if sz >= w:
                if sz == w:
                    self.fl.pop(i)
                else:
                    self.fl[i] = (o + w, sz - w)
                return o, w
        raise MemoryError(f"pool OOM want {w} have {self.fl}")

    def release(self, o, w):
        self.fl.append((o, w))
        self.fl.sort()
        merged = []
        for seg in self.fl:
            if merged and merged[-1][0] + merged[-1][1] == seg[0]:
                merged[-1] = (merged[-1][0], merged[-1][1] + seg[1])
            else:
                merged.append(list(seg))
        self.fl = [tuple(s) for s in merged]


class Al:
    def __init__(self, pf, pb, nf, nb):
        self.pools = {F32: Pool(pf, nf), BF: Pool(pb, nb)}
        self.meta = {}

    def sb(self, shape, dt, name=""):
        P, w = shape
        pool = self.pools[dt]
        o, wa = pool.alloc(w)
        h = pool.ap[0:P, o:o + w]
        self.meta[id(h)] = (pool, o, wa)
        return h

    def free(self, *hs):
        for h in hs:
            pool, o, wa = self.meta.pop(id(h))
            pool.release(o, wa)


def ln_tile(em, nc, out, x, tmp, st, F, eps_ap):
    v, s = nc.vector, nc.scalar
    em.op(v, 'reduce_sum', st[:, 0:1], x, axis=AX.X)
    em.op(v, 'tensor_scalar_mul', st[:, 1:2], st[:, 0:1], -1.0 / F)
    em.op(v, 'tensor_scalar_add', tmp, x, st[:, 1:2])
    em.op(s, 'activation', tmp, tmp, AF.Square)
    em.op(v, 'reduce_sum', st[:, 2:3], tmp, axis=AX.X)
    em.op(s, 'activation', st[:, 3:4], st[:, 2:3], AF.Sqrt, bias=eps_ap, scale=1.0 / F)
    em.op(v, 'reciprocal', st[:, 2:3], st[:, 3:4])
    em.op(v, 'tensor_scalar_add', tmp, x, st[:, 1:2])
    em.op(v, 'tensor_scalar_mul', out, tmp, st[:, 2:3])


def softmax_rows(em, nc, st, chunks):
    v, s = nc.vector, nc.scalar
    for i, (xc, _) in enumerate(chunks):
        em.op(v, 'reduce_max', st[:, 1:2] if i else st[:, 0:1], xc, axis=AX.X)
        if i:
            em.op(v, 'tensor_tensor', st[:, 0:1], st[:, 0:1], st[:, 1:2], ALU.max)
    em.op(v, 'tensor_scalar_mul', st[:, 0:1], st[:, 0:1], -1.0)
    for i, (xc, oc) in enumerate(chunks):
        em.op(s, 'activation', oc, xc, AF.Exp, bias=st[:, 0:1], scale=1.0)
        em.op(v, 'reduce_sum', st[:, 2:3] if i else st[:, 1:2], oc, axis=AX.X)
        if i:
            em.op(v, 'tensor_tensor', st[:, 1:2], st[:, 1:2], st[:, 2:3], ALU.add)
    em.op(v, 'reciprocal', st[:, 1:2], st[:, 1:2])
    for xc, oc in chunks:
        em.op(v, 'tensor_scalar_mul', oc, oc, st[:, 1:2])


def load_wT(em, nc, al, PS, w_ap, wT, eye, K, N, scale=None):
    ld = al.sb([128, K], F32, "wld")
    lb = al.sb([128, K], BF, "wldb")
    pt = PS['t'][:, 0:128]
    for i in range(N // 128):
        em.dma(ld[:, :], w_ap[i * 128:(i + 1) * 128, :])
        if scale is None:
            em.op(nc.vector, 'tensor_copy', lb[:, :], ld[:, :])
        else:
            em.op(nc.scalar, 'mul', lb[:, :], ld[:, :], scale)
        for j in range(K // 128):
            em.trans(pt, lb[:, j * 128:(j + 1) * 128], eye)
            em.op(nc.scalar, 'copy', wT[:, j * N + i * 128: j * N + (i + 1) * 128], pt)
    al.free(ld, lb)


def load_w(em, nc, al, w_ap, wsb, K, N):
    ld = al.sb([128, N], F32, "wld")
    for j in range(K // 128):
        em.dma(ld[:, :], w_ap[j * 128:(j + 1) * 128, :])
        em.op(nc.vector, 'tensor_copy', wsb[:, j * N:(j + 1) * N], ld[:, :])
    al.free(ld)


def bcast_row(em, nc, psum_f32, ones1, row_f32, N):
    em.mm(psum_f32[:, 0:N], ones1[0:1, :], row_f32[0:1, 0:N], start=True, stop=True)


def load_eye(em, nc, al, W):
    eye_f = al.sb([128, 128], F32, "eyef")
    eye = al.sb([128, 128], BF, "eye")
    em.dma(eye_f, W['eye'][:, :])
    em.op(nc.vector, 'tensor_copy', eye, eye_f)
    al.free(eye_f)
    return eye


def build_zT(em, nc, al, PS, x_ap, xoff, eye, zT, eps_ap):
    xt = al.sb([128, D], F32, "xt")
    tmp = al.sb([128, D], F32, "tmp")
    zb = al.sb([128, D], BF, "zb")
    st = al.sb([128, 4], F32, "st")
    pb = PS['t'][:, 0:128]
    for t in range(NT):
        em.dma(xt, x_ap[xoff + t * 128: xoff + (t + 1) * 128, :])
        ln_tile(em, nc, zb, xt, tmp, st, D, eps_ap)
        for k in range(NK):
            em.trans(pb, zb[:, k * 128:(k + 1) * 128], eye)
            em.op(nc.scalar, 'copy', zT[:, k * T + t * 128: k * T + (t + 1) * 128], pb)
    al.free(xt, tmp, zb, st)


def build_s1(nc, em, al, PS, A):
    H, dh = 16, 64
    x_ap, y_ap = A['x'], A['y']
    eye = load_eye(em, nc, al, A)
    eps = al.sb([128, 1], F32, "eps")
    em.op(nc.vector, 'memset', eps, 1e-5)

    wqT = al.sb([128, NK * D], BF, "wqT")
    wkT = al.sb([128, NK * D], BF, "wkT")
    wvT = al.sb([128, NK * D], BF, "wvT")
    woT = al.sb([128, NK * D], BF, "woT")
    for wT, nm in ((wqT, 'Wq'), (wkT, 'Wk'), (wvT, 'Wv'), (woT, 'Wo')):
        load_wT(em, nc, al, PS, A[nm], wT, eye, D, D)

    ps = PS['f']

    for b in range(BPC):
        xoff = b * T
        zT = al.sb([128, NK * T], BF, "zT")
        build_zT(em, nc, al, PS, x_ap, xoff, eye, zT, eps)

        qtok = al.sb([128, NT * D], BF, "qtok")
        ktok = al.sb([128, NT * D], BF, "ktok")
        for wT, dst in ((wqT, qtok), (wkT, ktok)):
            for t in range(NT):
                for n in range(2):
                    em.mm_acc(ps[0],
                              [(zT[:, k * T + t * 128: k * T + (t + 1) * 128],
                                wT[:, k * D + n * 512: k * D + (n + 1) * 512])
                               for k in range(NK)])
                    em.op(nc.scalar, 'copy', dst[:, t * D + n * 512: t * D + (n + 1) * 512], ps[0])

        zsm = al.sb([64, H * 64], BF, "zsm")
        zf = al.sb([64, 64], F32, "zf")
        st2 = al.sb([64, 4], F32, "st2")
        for h in range(H):
            em.mm_acc(ps[0][0:64, 0:64],
                      [(ktok[:, t * D + h * 64: t * D + (h + 1) * 64],
                        qtok[:, t * D + h * 64: t * D + (h + 1) * 64])
                       for t in range(NT)])
            em.op(nc.scalar, 'mul', zf, ps[0][0:64, 0:64], 1.0 / 8.0)
            softmax_rows(em, nc, st2, [(zf, zsm[:, h * 64:(h + 1) * 64])])
        al.free(qtok, ktok, zf, st2)

        o2T = al.sb([128, NK * T], BF, "o2T")
        vTh = al.sb([64, T], BF, "vTh")
        for h in range(H):
            for c in range(4):
                em.mm_acc(ps[0][0:64, :],
                          [(wvT[:, k * D + h * 64: k * D + (h + 1) * 64],
                            zT[:, k * T + c * 512: k * T + (c + 1) * 512])
                           for k in range(NK)])
                em.op(nc.scalar, 'copy', vTh[:, c * 512:(c + 1) * 512], ps[0][0:64, :])
            kt, ko = h // 2, (h % 2) * 64
            for c in range(4):
                em.mm(ps[1][0:64, :], zsm[:, h * 64:(h + 1) * 64],
                      vTh[:, c * 512:(c + 1) * 512], start=True, stop=True)
                em.op(nc.scalar, 'copy',
                      o2T[ko:ko + 64, kt * T + c * 512: kt * T + (c + 1) * 512],
                      ps[1][0:64, :])
        al.free(zT, vTh, zsm)

        xt = al.sb([128, D], F32, "xt2")
        yt = al.sb([128, D], F32, "yt")
        for t in range(NT):
            em.dma(xt, x_ap[xoff + t * 128: xoff + (t + 1) * 128, :])
            for n in range(2):
                em.mm_acc(ps[n],
                          [(o2T[:, k * T + t * 128: k * T + (t + 1) * 128],
                            woT[:, k * D + n * 512: k * D + (n + 1) * 512])
                           for k in range(NK)])
                em.op(nc.vector, 'tensor_tensor', yt[:, n * 512:(n + 1) * 512],
                      xt[:, n * 512:(n + 1) * 512], ps[n], ALU.add)
            em.dma(y_ap[xoff + t * 128: xoff + (t + 1) * 128, :], yt)
        al.free(o2T, xt, yt)
    al.free(wqT, wkT, wvT, woT, eye)


def build_s3(nc, em, al, PS, A):
    H, dh, W = 16, 64, 256
    x_ap, y_ap = A['x'], A['y']
    eye = load_eye(em, nc, al, A)
    eps = al.sb([128, 1], F32, "eps")
    em.op(nc.vector, 'memset', eps, 1e-5)
    ones1 = al.sb([1, 128], F32, "ones1")
    em.op(nc.vector, 'memset', ones1, 1.0)
    ps = PS['f']

    msk = al.sb([128, 4 * 512], F32, "msk")
    for i in range(4):
        em.dma(msk[:, i * 512:(i + 1) * 512], A['m3'][i * 128:(i + 1) * 128, :])

    vb = al.sb([128, D], F32, "vb")
    ob = al.sb([128, D], F32, "ob")
    row = al.sb([1, D], F32, "row")
    for bias_nm, dst in (('v_b', vb), ('o_b', ob)):
        em.dma(row, A[bias_nm][0:1, :])
        for n in range(2):
            bcast_row(em, nc, ps[0], ones1, row[:, n * 512:(n + 1) * 512], 512)
            em.op(nc.scalar, 'copy', dst[:, n * 512:(n + 1) * 512], ps[0])
    al.free(row)
    qb = al.sb([128, 8], F32, "qb")
    kb = al.sb([128, 8], F32, "kb")
    for bias_nm, dst in (('q_b', qb), ('k_b', kb)):
        for n in range(NK):
            em.dma(dst[:, n:n + 1], A[bias_nm][n * 128:(n + 1) * 128, :])

    for b in range(BPC):
        xoff = b * T
        zT = al.sb([128, NK * T], BF, "zT")
        build_zT(em, nc, al, PS, x_ap, xoff, eye, zT, eps)

        QT = al.sb([128, NK * T], BF, "QT")
        KT = al.sb([128, NK * T], BF, "KT")
        for wnm, bcol, dst in (('q_w', qb, QT), ('k_w', kb, KT)):
            wT = al.sb([128, NK * D], BF, "wT")
            load_wT(em, nc, al, PS, A[wnm], wT, eye, D, D)
            for n in range(NK):
                for c in range(4):
                    em.mm_acc(ps[0],
                              [(wT[:, k * D + n * 128: k * D + (n + 1) * 128],
                                zT[:, k * T + c * 512: k * T + (c + 1) * 512])
                               for k in range(NK)])
                    em.op(nc.scalar, 'activation',
                          dst[:, n * T + c * 512: n * T + (c + 1) * 512], ps[0],
                          AF.Identity, bias=bcol[:, n:n + 1], scale=1.0)
            al.free(wT)
        V = al.sb([128, NT * D], BF, "V")
        wT = al.sb([128, NK * D], BF, "wT")
        load_wT(em, nc, al, PS, A['v_w'], wT, eye, D, D)
        for t in range(NT):
            for n in range(2):
                em.mm_acc(ps[0],
                          [(zT[:, k * T + t * 128: k * T + (t + 1) * 128],
                            wT[:, k * D + n * 512: k * D + (n + 1) * 512])
                           for k in range(NK)])
                em.op(nc.vector, 'tensor_tensor',
                      V[:, t * D + n * 512: t * D + (n + 1) * 512],
                      ps[0], vb[:, n * 512:(n + 1) * 512], ALU.add)
        al.free(wT, zT)

        oT = al.sb([128, NK * T], BF, "oT")
        sf = al.sb([128, 512], F32, "sf")
        p = al.sb([128, 512], BF, "p")
        pT = al.sb([128, 512], BF, "pT")
        st = al.sb([128, 4], F32, "st")
        ot = al.sb([128, D], BF, "ot")
        for qc in range(NT):
            start = (qc // 2 - 1) * 256
            mi = qc if qc < 2 else 2 + (qc & 1)
            for h in range(H):
                kt, ko = h // 2, (h % 2) * 64
                qv = QT[kt * 128 + ko: kt * 128 + ko + 64,
                        kt * 0 + qc * 128: qc * 128 + 128] if False else None
                lhs = QT[ko:ko + 64, kt * T + qc * 128: kt * T + (qc + 1) * 128]
                if qc < 2:
                    em.mm(ps[0][:, 256:512], lhs, KT[ko:ko + 64, kt * T: kt * T + 256],
                          start=True, stop=True)
                else:
                    em.mm(ps[0], lhs, KT[ko:ko + 64, kt * T + start: kt * T + start + 512],
                          start=True, stop=True)
                em.op(nc.vector, 'scalar_tensor_tensor', sf, ps[0], 0.125,
                      msk[:, mi * 512:(mi + 1) * 512], ALU.mult, ALU.add)
                softmax_rows(em, nc, st, [(sf, p)])
                jjs = [2, 3] if qc < 2 else [0, 1, 2, 3]
                for jj in jjs:
                    em.trans(PS['t'][:, 0:128], p[:, jj * 128:(jj + 1) * 128], eye)
                    em.op(nc.scalar, 'copy', pT[:, jj * 128:(jj + 1) * 128], PS['t'][:, 0:128])
                em.mm_acc(ps[1][:, 0:64],
                          [(pT[:, jj * 128:(jj + 1) * 128],
                            V[:, (start // 128 + jj) * D + h * 64:
                              (start // 128 + jj) * D + (h + 1) * 64])
                           for jj in jjs])
                em.op(nc.scalar, 'copy', ot[:, h * 64:(h + 1) * 64], ps[1][:, 0:64])
            for k in range(NK):
                em.trans(PS['t'][:, 0:128], ot[:, k * 128:(k + 1) * 128], eye)
                em.op(nc.scalar, 'copy', oT[:, k * T + qc * 128: k * T + (qc + 1) * 128],
                      PS['t'][:, 0:128])
        al.free(QT, KT, V, sf, p, pT, st, ot)

        woT = al.sb([128, NK * D], BF, "woT")
        load_wT(em, nc, al, PS, A['o_w'], woT, eye, D, D)
        xt = al.sb([128, D], F32, "xt2")
        yt = al.sb([128, D], F32, "yt")
        for t in range(NT):
            em.dma(xt, x_ap[xoff + t * 128: xoff + (t + 1) * 128, :])
            for n in range(2):
                em.mm_acc(ps[n],
                          [(oT[:, k * T + t * 128: k * T + (t + 1) * 128],
                            woT[:, k * D + n * 512: k * D + (n + 1) * 512])
                           for k in range(NK)])
                em.op(nc.vector, 'tensor_tensor', yt[:, n * 512:(n + 1) * 512],
                      ps[n], ob[:, n * 512:(n + 1) * 512], ALU.add)
            em.op(nc.vector, 'tensor_tensor', yt, yt, xt, ALU.add)
            em.dma(y_ap[xoff + t * 128: xoff + (t + 1) * 128, :], yt)
        al.free(woT, oT, xt, yt)
    al.free(msk, vb, ob, qb, kb, eye, eps, ones1)


def s3_masks():
    m = np.zeros((4, 128, 512), np.float32)
    il = np.arange(128)[:, None]
    jl = np.arange(512)[None, :]
    for mi, off in ((2, 256), (3, 384)):
        bad = (jl < il + (off - 256)) | (jl > il + off)
        m[mi][bad] = -1e30
    m[0] = m[2]; m[0][:, (jl[0] < 256)] = -1e30
    m[1] = m[3]; m[1][:, (jl[0] < 256)] = -1e30
    return m.reshape(4 * 128, 512)


S3_TENSORS = [('x', (BPC * T, D), 'ExternalInput'),
              ('y', (BPC * T, D), 'ExternalOutput'),
              ('q_w', (D, D), 'ExternalInput'),
              ('k_w', (D, D), 'ExternalInput'),
              ('v_w', (D, D), 'ExternalInput'),
              ('o_w', (D, D), 'ExternalInput'),
              ('q_b', (D, 1), 'ExternalInput'),
              ('k_b', (D, 1), 'ExternalInput'),
              ('v_b', (1, D), 'ExternalInput'),
              ('o_b', (1, D), 'ExternalInput'),
              ('m3', (4 * 128, 512), 'ExternalInput'),
              ('eye', (128, 128), 'ExternalInput')]




def build_s5(nc, em, al, PS, A):
    H, DK, M = 16, 64, 128
    EPS = 1e-6
    RM = 1.0 / math.sqrt(M)
    XSC = 1.0 / DK ** 0.25
    x_ap, y_ap, osc = A['x'], A['y'], A['osc']
    eye = load_eye(em, nc, al, A)
    eps = al.sb([128, 1], F32, "eps")
    em.op(nc.vector, 'memset', eps, 1e-5)
    ones1 = al.sb([1, 128], F32, "ones1")
    em.op(nc.vector, 'memset', ones1, 1.0)
    ones64 = al.sb([64, 1], BF, "ones64")
    em.op(nc.vector, 'memset', ones64, 1.0)
    ps = PS['f']

    projT = al.sb([128, 128], BF, "projT")
    pl = al.sb([128, 64], F32, "pl")
    plb = al.sb([128, 64], BF, "plb")
    em.dma(pl, A['proj'][:, :])
    em.op(nc.vector, 'tensor_copy', plb, pl)
    em.trans(PS['t'][0:64, 0:128], plb, eye)
    em.op(nc.scalar, 'copy', projT[0:64, :], PS['t'][0:64, 0:128])
    em.op(nc.scalar, 'copy', projT[64:128, :], PS['t'][0:64, 0:128])
    al.free(pl, plb)

    mk = al.sb([128, 128], F32, "mk")
    em.dma(mk, A['m5'][:, :])

    vb = al.sb([128, D], F32, "vb")
    ob = al.sb([128, D], F32, "ob")
    row = al.sb([1, D], F32, "row")
    for bias_nm, dst in (('bv', vb), ('bo', ob)):
        em.dma(row, A[bias_nm][0:1, :])
        for n in range(2):
            bcast_row(em, nc, ps[0], ones1, row[:, n * 512:(n + 1) * 512], 512)
            em.op(nc.scalar, 'copy', dst[:, n * 512:(n + 1) * 512], ps[0])
    al.free(row)
    qb = al.sb([128, 8], F32, "qb")
    kb = al.sb([128, 8], F32, "kb")
    for bias_nm, dst in (('bq', qb), ('bk', kb)):
        for n in range(NK):
            em.dma(dst[:, n:n + 1], A[bias_nm][n * 128:(n + 1) * 128, :])

    for b in range(BPC):
        xoff = b * T
        zT = al.sb([128, NK * T], BF, "zT")
        build_zT(em, nc, al, PS, x_ap, xoff, eye, zT, eps)

        QTs = al.sb([128, NK * T], BF, "QTs")
        KTs = al.sb([128, NK * T], BF, "KTs")
        tmpf = al.sb([128, 512], F32, "tmpf")
        for wnm, bcol, dst in (('Wq', qb, QTs), ('Wk', kb, KTs)):
            wT = al.sb([128, NK * D], BF, "wT")
            load_wT(em, nc, al, PS, A[wnm], wT, eye, D, D)
            for n in range(NK):
                for c in range(4):
                    em.mm_acc(ps[0],
                              [(wT[:, k * D + n * 128: k * D + (n + 1) * 128],
                                zT[:, k * T + c * 512: k * T + (c + 1) * 512])
                               for k in range(NK)])
                    em.op(nc.scalar, 'activation', tmpf, ps[0],
                          AF.Identity, bias=bcol[:, n:n + 1], scale=1.0)
                    em.op(nc.scalar, 'mul',
                          dst[:, n * T + c * 512: n * T + (c + 1) * 512], tmpf, XSC)
            al.free(wT)
        al.free(tmpf)
        va = al.sb([128, NT * H * 65], BF, "va")
        wT = al.sb([128, NK * D], BF, "wT")
        load_wT(em, nc, al, PS, A['Wv'], wT, eye, D, D)
        for t in range(NT):
            base = t * H * 65
            for n in range(2):
                em.mm_acc(ps[n],
                          [(zT[:, k * T + t * 128: k * T + (t + 1) * 128],
                            wT[:, k * D + n * 512: k * D + (n + 1) * 512])
                           for k in range(NK)])
            for h in range(H):
                pn, off = divmod(h * 64, 512)
                em.op(nc.vector, 'tensor_tensor',
                      va[:, base + h * 65: base + h * 65 + 64],
                      ps[pn][:, off:off + 64], vb[:, h * 64:(h + 1) * 64], ALU.add)
            em.op(nc.vector, 'memset',
                  va[:, base:base + H * 65].rearrange("p (h c) -> p h c", c=65)[:, :, 64:65],
                  1.0)
        al.free(wT, zT)

        qfT = al.sb([128, T], BF, "qfT")
        kfT = al.sb([128, T], BF, "kfT")
        kftok = al.sb([128, T], BF, "kftok")
        SQ = al.sb([64, 512], BF, "SQ")
        srow = al.sb([1, 512], F32, "srow")
        uf = al.sb([128, 512], F32, "uf")
        S_run = al.sb([128, 65], F32, "S_run")
        S_bf = al.sb([128, 65], BF, "S_bf")
        atm = al.sb([128, 128], BF, "atm")
        dr = al.sb([128, 1], F32, "dr")
        ot = al.sb([128, 64], F32, "ot")
        for h in range(H):
            kt, ko = h // 2, (h % 2) * 64
            for src, dst in ((QTs, qfT), (KTs, kfT)):
                for c in range(4):
                    sl = src[ko:ko + 64, kt * T + c * 512: kt * T + (c + 1) * 512]
                    em.op(nc.scalar, 'activation', SQ, sl, AF.Square)
                    em.mm(ps[2][0:1, :], ones64, SQ, start=True, stop=True)
                    em.op(nc.scalar, 'mul', srow, ps[2][0:1, :], 0.5)
                    em.mm(ps[0], projT[ko:ko + 64, :], sl, start=True, stop=True)
                    em.op(nc.scalar, 'copy', uf, ps[0])
                    bcast_row(em, nc, ps[1], ones1, srow, 512)
                    em.op(nc.vector, 'tensor_tensor', uf, uf, ps[1], ALU.subtract)
                    em.op(nc.scalar, 'activation', uf, uf, AF.Exp)
                    em.op(nc.vector, 'tensor_scalar',
                          dst[:, c * 512:(c + 1) * 512], uf, RM, EPS, ALU.mult, ALU.add)
            for n in range(NT):
                em.trans(PS['t'][:, 0:128], kfT[:, n * 128:(n + 1) * 128], eye)
                em.op(nc.scalar, 'copy', kftok[:, n * 128:(n + 1) * 128], PS['t'][:, 0:128])
            em.op(nc.vector, 'memset', S_run, 0.0)
            em.op(nc.vector, 'memset', S_bf, 0.0)
            for n in range(NT):
                vsl = va[:, n * H * 65 + h * 65: n * H * 65 + h * 65 + 65]
                em.mm(ps[0][:, 0:65], kftok[:, n * 128:(n + 1) * 128], vsl,
                      start=True, stop=True)
                em.mm(ps[1][:, 0:128], kfT[:, n * 128:(n + 1) * 128],
                      qfT[:, n * 128:(n + 1) * 128], start=True, stop=True)
                em.op(nc.vector, 'tensor_tensor', atm, ps[1][:, 0:128], mk, ALU.mult)
                em.mm(ps[2][:, 0:65], qfT[:, n * 128:(n + 1) * 128], S_bf,
                      start=True, stop=False)
                em.mm(ps[2][:, 0:65], atm, vsl, start=False, stop=True, chain=True)
                em.op(nc.vector, 'tensor_scalar_add', dr, ps[2][:, 64:65], EPS)
                em.op(nc.vector, 'reciprocal', dr, dr)
                em.op(nc.vector, 'tensor_scalar_mul', ot, ps[2][:, 0:64], dr)
                em.dma(osc[xoff + n * 128: xoff + (n + 1) * 128, h * 64:(h + 1) * 64], ot)
                em.op(nc.vector, 'tensor_tensor', S_run, S_run, ps[0][:, 0:65], ALU.add)
                em.op(nc.vector, 'tensor_copy', S_bf, S_run)
        al.free(QTs, KTs, va, qfT, kfT, kftok, SQ, srow, uf, S_run, S_bf, atm, dr, ot)

        woT = al.sb([128, NK * D], BF, "woT")
        load_wT(em, nc, al, PS, A['Wo'], woT, eye, D, D)
        oT = al.sb([128, NK * T], BF, "oT")
        ol = al.sb([128, D], F32, "ol")
        olb = al.sb([128, D], BF, "olb")
        for t in range(NT):
            em.dma(ol, osc[xoff + t * 128: xoff + (t + 1) * 128, :])
            em.op(nc.vector, 'tensor_copy', olb, ol)
            for k in range(NK):
                em.trans(PS['t'][:, 0:128], olb[:, k * 128:(k + 1) * 128], eye)
                em.op(nc.scalar, 'copy', oT[:, k * T + t * 128: k * T + (t + 1) * 128],
                      PS['t'][:, 0:128])
        xt = al.sb([128, D], F32, "xt2")
        yt = al.sb([128, D], F32, "yt")
        for t in range(NT):
            em.dma(xt, x_ap[xoff + t * 128: xoff + (t + 1) * 128, :])
            for n in range(2):
                em.mm_acc(ps[n],
                          [(oT[:, k * T + t * 128: k * T + (t + 1) * 128],
                            woT[:, k * D + n * 512: k * D + (n + 1) * 512])
                           for k in range(NK)])
                em.op(nc.vector, 'tensor_tensor', yt[:, n * 512:(n + 1) * 512],
                      ps[n], ob[:, n * 512:(n + 1) * 512], ALU.add)
            em.op(nc.vector, 'tensor_tensor', yt, yt, xt, ALU.add)
            em.dma(y_ap[xoff + t * 128: xoff + (t + 1) * 128, :], yt)
        al.free(woT, oT, ol, olb, xt, yt)
    al.free(eye, eps, ones1, ones64, projT, mk, vb, ob, qb, kb)


def s5_mask():
    e = np.arange(128)[:, None]
    c = np.arange(128)[None, :]
    return (e <= c).astype(np.float32)


S5_TENSORS = [('x', (BPC * T, D), 'ExternalInput'),
              ('y', (BPC * T, D), 'ExternalOutput'),
              ('osc', (BPC * T, D), 'Internal'),
              ('Wq', (D, D), 'ExternalInput'),
              ('Wk', (D, D), 'ExternalInput'),
              ('Wv', (D, D), 'ExternalInput'),
              ('Wo', (D, D), 'ExternalInput'),
              ('bq', (D, 1), 'ExternalInput'),
              ('bk', (D, 1), 'ExternalInput'),
              ('bv', (1, D), 'ExternalInput'),
              ('bo', (1, D), 'ExternalInput'),
              ('proj', (128, 64), 'ExternalInput'),
              ('m5', (128, 128), 'ExternalInput'),
              ('eye', (128, 128), 'ExternalInput')]


def build_s7(nc, em, al, PS, A):
    H, DH, DHR, DL = 16, 64, 32, 256
    SCL = 1.0 / math.sqrt(96.0)
    x_ap, y_ap = A['x'], A['y']
    eye = load_eye(em, nc, al, A)
    eps = al.sb([128, 1], F32, "eps")
    em.op(nc.vector, 'memset', eps, 1e-5)
    ones1 = al.sb([1, 128], F32, "ones1")
    em.op(nc.vector, 'memset', ones1, 1.0)
    ps = PS['f']

    m7 = al.sb([128, 4 * 512], F32, "m7")
    for i in range(4):
        em.dma(m7[:, i * 512:(i + 1) * 512], A['m7'][i * 128:(i + 1) * 128, :])

    wdq = al.sb([128, NK * 256], BF, "wdq")
    wdkv = al.sb([128, NK * 256], BF, "wdkv")
    load_w(em, nc, al, A['W_DQ'], wdq, D, 256)
    load_w(em, nc, al, A['W_DKV'], wdkv, D, 256)
    wkr = al.sb([128, NK * 32], BF, "wkr")
    load_w(em, nc, al, A['W_KR'], wkr, D, 32)
    wuq = al.sb([128, 2 * D], BF, "wuq")
    wuk = al.sb([128, 2 * D], BF, "wuk")
    wuv = al.sb([128, 2 * D], BF, "wuv")
    wqr = al.sb([128, 2 * 512], BF, "wqr")
    load_w(em, nc, al, A['W_UQ'], wuq, 256, D)
    load_w(em, nc, al, A['W_UK'], wuk, 256, D)
    load_w(em, nc, al, A['W_UV'], wuv, 256, D)
    load_w(em, nc, al, A['W_QR'], wqr, 256, 512)

    ct = al.sb([128, 256], F32, "ct")
    st7 = al.sb([128, 256], F32, "st7")
    swp = al.sb([128, 256], F32, "swp")
    rf = al.sb([128, 256], F32, "rf")
    rb = al.sb([128, 256], BF, "rb")

    def rope_tile(em, q_f32, hcnt, t):
        em.dma(ct[:, 0:hcnt * 32], A['ct7'][t * 128:(t + 1) * 128, 0:hcnt * 32])
        em.dma(st7[:, 0:hcnt * 32], A['st7t'][t * 128:(t + 1) * 128, 0:hcnt * 32])
        qv = q_f32.rearrange("p (h a two) -> p h a two", two=2, a=16)
        sv = swp[:, 0:hcnt * 32].rearrange("p (h a two) -> p h a two", two=2, a=16)
        em.op(nc.vector, 'tensor_scalar_mul', sv[:, :, :, 0:1], qv[:, :, :, 1:2], -1.0)
        em.op(nc.vector, 'tensor_copy', sv[:, :, :, 1:2], qv[:, :, :, 0:1])
        em.op(nc.vector, 'tensor_tensor', q_f32, q_f32, ct[:, 0:hcnt * 32], ALU.mult)
        em.op(nc.vector, 'tensor_tensor', swp[:, 0:hcnt * 32], swp[:, 0:hcnt * 32],
              st7[:, 0:hcnt * 32], ALU.mult)
        em.op(nc.vector, 'tensor_tensor', q_f32, q_f32, swp[:, 0:hcnt * 32], ALU.add)

    for b in range(BPC):
        xoff = b * T
        zT = al.sb([128, NK * T], BF, "zT")
        build_zT(em, nc, al, PS, x_ap, xoff, eye, zT, eps)

        cqT = al.sb([128, 2 * T], BF, "cqT")
        ckvT = al.sb([128, 2 * T], BF, "ckvT")
        for wsb, dst in ((wdq, cqT), (wdkv, ckvT)):
            for n in range(2):
                for c in range(4):
                    em.mm_acc(ps[0],
                              [(wsb[:, k * 256 + n * 128: k * 256 + (n + 1) * 128],
                                zT[:, k * T + c * 512: k * T + (c + 1) * 512])
                               for k in range(NK)])
                    em.op(nc.scalar, 'copy', dst[:, n * T + c * 512: n * T + (c + 1) * 512],
                          ps[0])
        krT = al.sb([64, T], BF, "krT")
        for t in range(NT):
            em.mm_acc(ps[0][:, 0:32],
                      [(zT[:, k * T + t * 128: k * T + (t + 1) * 128],
                        wkr[:, k * 32:(k + 1) * 32]) for k in range(NK)])
            em.op(nc.scalar, 'copy', rf[:, 0:32], ps[0][:, 0:32])
            rope_tile(em, rf[:, 0:32], 1, t)
            em.op(nc.vector, 'tensor_copy', rb[:, 0:32], rf[:, 0:32])
            em.trans(PS['t'][0:32, 0:128], rb[:, 0:32], eye)
            em.op(nc.scalar, 'copy', krT[0:32, t * 128:(t + 1) * 128], PS['t'][0:32, 0:128])
            em.op(nc.scalar, 'copy', krT[32:64, t * 128:(t + 1) * 128], PS['t'][0:32, 0:128])
        al.free(zT)

        aoT = al.sb([128, NK * T], BF, "aoT")
        expT = al.sb([128, NT * 512], BF, "expT")
        sf = al.sb([128, 512], F32, "sf")
        dsb = al.sb([64, 512], F32, "dsb")
        drow = al.sb([1, 512], F32, "drow")
        for grp in range(2):
            QcT = al.sb([128, 4 * T], BF, "QcT")
            KcT = al.sb([128, 4 * T], BF, "KcT")
            QrT = al.sb([64, 4 * T], BF, "QrT")
            Vg = al.sb([128, NT * 8 * 65], BF, "Vg")
            for wsb, cT, dst in ((wuq, cqT, QcT), (wuk, ckvT, KcT)):
                for n in range(4):
                    nn = grp * 4 + n
                    for c in range(4):
                        em.mm_acc(ps[0],
                                  [(wsb[:, k * D + nn * 128: k * D + (nn + 1) * 128],
                                    cT[:, k * T + c * 512: k * T + (c + 1) * 512])
                                   for k in range(2)])
                        em.op(nc.scalar, 'copy',
                              dst[:, n * T + c * 512: n * T + (c + 1) * 512], ps[0])
            for t in range(NT):
                em.mm_acc(ps[0][:, 0:256],
                          [(cqT[:, k * T + t * 128: k * T + (t + 1) * 128],
                            wqr[:, k * 512 + grp * 256: k * 512 + grp * 256 + 256])
                           for k in range(2)])
                em.op(nc.scalar, 'copy', rf, ps[0][:, 0:256])
                rope_tile(em, rf, 8, t)
                em.op(nc.vector, 'tensor_copy', rb, rf)
                for hh in range(8):
                    em.trans(PS['t'][0:32, 0:128], rb[:, hh * 32:(hh + 1) * 32], eye)
                    em.op(nc.scalar, 'copy',
                          QrT[(hh % 2) * 32:(hh % 2) * 32 + 32,
                              (hh // 2) * T + t * 128:(hh // 2) * T + (t + 1) * 128],
                          PS['t'][0:32, 0:128])
            for t in range(NT):
                base = t * 8 * 65
                em.mm_acc(ps[0],
                          [(ckvT[:, k * T + t * 128: k * T + (t + 1) * 128],
                            wuv[:, k * D + grp * 512: k * D + grp * 512 + 512])
                           for k in range(2)])
                for hh in range(8):
                    em.op(nc.scalar, 'copy', Vg[:, base + hh * 65: base + hh * 65 + 64],
                          ps[0][:, hh * 64:(hh + 1) * 64])
                em.op(nc.vector, 'memset',
                      Vg[:, base:base + 8 * 65].rearrange("p (h c) -> p h c", c=65)[:, :, 64:65],
                      1.0)
            for hh in range(8):
                h = grp * 8 + hh
                co, ro, r32 = hh // 2, (hh % 2) * 64, (hh % 2) * 32
                for ic in range(4):
                    jcs = list(range(ic * 4 + 4))
                    for jc in jcs:
                        em.mm(ps[0], KcT[ro:ro + 64, co * T + jc * 128: co * T + (jc + 1) * 128],
                              QcT[ro:ro + 64, co * T + ic * 512: co * T + (ic + 1) * 512],
                              start=True, stop=False)
                        em.mm(ps[0], krT[r32:r32 + 32, jc * 128:(jc + 1) * 128],
                              QrT[r32:r32 + 32, co * T + ic * 512: co * T + (ic + 1) * 512],
                              start=False, stop=True, chain=True)
                        em.op(nc.vector, 'tensor_scalar', sf, ps[0], 80.0, -80.0,
                              ALU.min, ALU.max)
                        em.op(nc.scalar, 'activation', expT[:, jc * 512:(jc + 1) * 512],
                              sf, AF.Exp, scale=SCL)
                        if jc >= ic * 4:
                            r = jc - ic * 4
                            em.op(nc.vector, 'tensor_tensor',
                                  expT[:, jc * 512:(jc + 1) * 512],
                                  expT[:, jc * 512:(jc + 1) * 512],
                                  m7[:, r * 512:(r + 1) * 512], ALU.mult)
                    em.mm_acc(ps[1][0:65, :],
                              [(Vg[:, jc * 8 * 65 + hh * 65: jc * 8 * 65 + hh * 65 + 65],
                                expT[:, jc * 512:(jc + 1) * 512]) for jc in jcs])
                    em.op(nc.vector, 'reciprocal', drow, ps[1][64:65, :])
                    bcast_row(em, nc, ps[2][0:64, :], ones1[0:1, 0:64], drow, 512)
                    em.op(nc.scalar, 'copy', dsb, ps[2][0:64, :])
                    em.op(nc.vector, 'tensor_tensor',
                          aoT[ro:ro + 64, (h // 2) * T + ic * 512:(h // 2) * T + (ic + 1) * 512],
                          ps[1][0:64, :], dsb, ALU.mult)
            al.free(QcT, KcT, QrT, Vg)
        al.free(cqT, ckvT, krT, expT, sf, dsb, drow)

        wo = al.sb([128, NK * D], BF, "wo")
        load_w(em, nc, al, A['W_O'], wo, D, D)
        xt = al.sb([128, D], F32, "xt2")
        yt = al.sb([128, D], F32, "yt")
        for t in range(NT):
            em.dma(xt, x_ap[xoff + t * 128: xoff + (t + 1) * 128, :])
            for n in range(2):
                em.mm_acc(ps[n],
                          [(aoT[:, k * T + t * 128: k * T + (t + 1) * 128],
                            wo[:, k * D + n * 512: k * D + (n + 1) * 512])
                           for k in range(NK)])
                em.op(nc.vector, 'tensor_tensor', yt[:, n * 512:(n + 1) * 512],
                      xt[:, n * 512:(n + 1) * 512], ps[n], ALU.add)
            em.dma(y_ap[xoff + t * 128: xoff + (t + 1) * 128, :], yt)
        al.free(wo, xt, yt, aoT)
    al.free(eye, eps, ones1, m7, wdq, wdkv, wkr, wuq, wuk, wuv, wqr, ct, st7, swp, rf, rb)


def s7_tables():
    dd = 32
    inv = 10000.0 ** (-np.arange(0, dd, 2, dtype=np.float64) / dd)
    t = np.arange(T, dtype=np.float64)
    ang = t[:, None] * inv[None, :]
    c2 = np.repeat(np.cos(ang), 2, axis=1)
    s2 = np.repeat(np.sin(ang), 2, axis=1)
    ct = np.tile(c2, (1, 8)).astype(np.float32)
    st = np.tile(s2, (1, 8)).astype(np.float32)
    jl = np.arange(128)[:, None]
    il = np.arange(512)[None, :]
    m = np.zeros((4, 128, 512), np.float32)
    for r in range(4):
        m[r] = (il >= jl + r * 128).astype(np.float32)
    return ct, st, m.reshape(4 * 128, 512)


S7_TENSORS = [('x', (BPC * T, D), 'ExternalInput'),
              ('y', (BPC * T, D), 'ExternalOutput'),
              ('W_DQ', (D, 256), 'ExternalInput'),
              ('W_UQ', (256, D), 'ExternalInput'),
              ('W_QR', (256, 512), 'ExternalInput'),
              ('W_DKV', (D, 256), 'ExternalInput'),
              ('W_UK', (256, D), 'ExternalInput'),
              ('W_UV', (256, D), 'ExternalInput'),
              ('W_KR', (D, 32), 'ExternalInput'),
              ('W_O', (D, D), 'ExternalInput'),
              ('ct7', (T, 256), 'ExternalInput'),
              ('st7t', (T, 256), 'ExternalInput'),
              ('m7', (4 * 128, 512), 'ExternalInput'),
              ('eye', (128, 128), 'ExternalInput')]



def build_s2(nc, em, al, PS, A):
    Hh, DH, W2S = 8, 64, 512
    HID = 2730
    x_ap, y_ap, usc = A['x'], A['y'], A['usc']
    eye = load_eye(em, nc, al, A)
    eps = al.sb([128, 1], F32, "eps")
    em.op(nc.vector, 'memset', eps, 1e-5)
    ones1 = al.sb([1, 128], F32, "ones1")
    em.op(nc.vector, 'memset', ones1, 1.0)
    ps = PS['f']
    row = al.sb([1, D], F32, "row")
    msk = al.sb([128, 8 * 512], F32, "msk2")
    for i in range(8):
        em.dma(msk[:, i * 512:(i + 1) * 512], A['m2'][i * 128:(i + 1) * 128, :])

    for b in range(BPC):
        xoff = b * T
        xt = al.sb([128, D], F32, "xt")
        tmp = al.sb([128, D], F32, "tmp")
        st = al.sb([128, 8], F32, "st")
        for t in range(NT):
            em.dma(xt, x_ap[xoff + t * 128: xoff + (t + 1) * 128, :])
            ln_tile(em, nc, tmp, xt, tmp, st, D, eps)
            em.dma(usc[xoff + t * 128: xoff + (t + 1) * 128, :], tmp)
        al.free(xt, tmp, st)

        for l in range(2):
            wqkv = al.sb([128, NK * 1536], BF, "wqkv")
            load_w(em, nc, al, A['wqkv'][l * D:(l + 1) * D, :], wqkv, D, 1536)
            QS = al.sb([128, 512], F32, "QS")
            KS = al.sb([128, 512], F32, "KS")
            for nm, dst in (('qsr', QS), ('ksr', KS)):
                em.dma(row[:, 0:512], A[nm][l:l + 1, :])
                bcast_row(em, nc, ps[0], ones1, row[:, 0:512], 512)
                em.op(nc.scalar, 'copy', dst, ps[0])
            vtok = al.sb([128, NT * 512], BF, "vtok")
            qT = al.sb([128, 4 * T], BF, "qT")
            kT = al.sb([128, 4 * T], BF, "kT")
            hTt = al.sb([128, NK * 128], BF, "hTt")
            xt = al.sb([128, D], F32, "xt")
            sh = al.sb([128, 512], F32, "sh")
            hb = al.sb([128, D], BF, "hb")
            qf = al.sb([128, 512], F32, "qf")
            sq = al.sb([128, 512], F32, "sq")
            ss = al.sb([128, 8], F32, "ss")
            at = al.sb([128, 512], F32, "at")
            bt = al.sb([128, 512], F32, "bt")
            swp = al.sb([128, 512], F32, "swp")
            qb2 = al.sb([128, 512], BF, "qb2")
            for t in range(NT):
                em.dma(xt, usc[xoff + t * 128: xoff + (t + 1) * 128, :])
                if t == 0:
                    em.op(nc.vector, 'memset', sh[0:1, :], 0.0)
                    em.dma(sh[1:128, :], usc[xoff: xoff + 127, 512:1024])
                else:
                    em.dma(sh, usc[xoff + t * 128 - 1: xoff + (t + 1) * 128 - 1, 512:1024])
                em.op(nc.vector, 'tensor_copy', hb[:, 0:512], xt[:, 0:512])
                em.op(nc.vector, 'tensor_copy', hb[:, 512:1024], sh)
                for k in range(NK):
                    em.trans(PS['t'][:, 0:128], hb[:, k * 128:(k + 1) * 128], eye)
                    em.op(nc.scalar, 'copy', hTt[:, k * 128:(k + 1) * 128], PS['t'][:, 0:128])
                em.mm_acc(ps[0], [(hTt[:, k * 128:(k + 1) * 128],
                                   wqkv[:, k * 1536 + 1024: k * 1536 + 1536])
                                  for k in range(NK)])
                em.op(nc.vector, 'tensor_copy', vtok[:, t * 512:(t + 1) * 512], ps[0])
                for idx, (SSC, TA, TB, dstT) in enumerate(
                        ((QS, 'aq', 'bq', qT), (KS, 'ak', 'bk', kT))):
                    em.mm_acc(ps[1], [(hTt[:, k * 128:(k + 1) * 128],
                                       wqkv[:, k * 1536 + idx * 512: k * 1536 + (idx + 1) * 512])
                                      for k in range(NK)])
                    em.op(nc.scalar, 'copy', qf, ps[1])
                    em.op(nc.scalar, 'activation', sq, qf, AF.Square)
                    for h in range(Hh):
                        em.op(nc.vector, 'reduce_sum', ss[:, h:h + 1],
                              sq[:, h * 64:(h + 1) * 64], axis=AX.X)
                    em.op(nc.vector, 'tensor_scalar_max', ss, ss, 1e-24)
                    em.op(nc.scalar, 'activation', ss, ss, AF.Sqrt)
                    em.op(nc.vector, 'reciprocal', ss, ss)
                    for h in range(Hh):
                        em.op(nc.vector, 'tensor_scalar_mul', qf[:, h * 64:(h + 1) * 64],
                              qf[:, h * 64:(h + 1) * 64], ss[:, h:h + 1])
                    em.op(nc.vector, 'tensor_tensor', qf, qf, SSC, ALU.mult)
                    em.dma(at, A[TA][t * 128:(t + 1) * 128, :])
                    em.dma(bt, A[TB][t * 128:(t + 1) * 128, :])
                    qv = qf.rearrange("p (h s) -> p h s", s=64)
                    sv = swp.rearrange("p (h s) -> p h s", s=64)
                    em.op(nc.vector, 'tensor_scalar_mul', sv[:, :, 0:32], qv[:, :, 32:64], -1.0)
                    em.op(nc.vector, 'tensor_copy', sv[:, :, 32:64], qv[:, :, 0:32])
                    em.op(nc.vector, 'tensor_tensor', qf, qf, at, ALU.mult)
                    em.op(nc.vector, 'tensor_tensor', swp, swp, bt, ALU.mult)
                    em.op(nc.vector, 'tensor_tensor', qf, qf, swp, ALU.add)
                    em.op(nc.vector, 'tensor_copy', qb2, qf)
                    for k in range(4):
                        em.trans(PS['t'][:, 0:128], qb2[:, k * 128:(k + 1) * 128], eye)
                        em.op(nc.scalar, 'copy',
                              dstT[:, k * T + t * 128: k * T + (t + 1) * 128],
                              PS['t'][:, 0:128])
            al.free(hTt, xt, sh, hb, qf, sq, ss, at, bt, swp, qb2, wqkv, QS, KS)
            otok = al.sb([128, NT * 512], BF, "otok")
            sfA = al.sb([128, 512], F32, "sfA")
            sfB = al.sb([128, 512], F32, "sfB")
            p2 = al.sb([128, 1024], BF, "p2")
            pT = al.sb([128, 1024], BF, "pT")
            st = al.sb([128, 4], F32, "st")
            for qc in range(NT):
                blk = qc // 4
                r = qc % 4
                start = (blk - 1) * 512
                for h in range(Hh):
                    co, ro = h // 2, (h % 2) * 64
                    chunks = []
                    if blk > 0:
                        em.mm(ps[0], qT[ro:ro + 64, co * T + qc * 128: co * T + (qc + 1) * 128],
                              kT[ro:ro + 64, co * T + start: co * T + start + 512],
                              start=True, stop=True)
                        em.op(nc.vector, 'scalar_tensor_tensor', sfA, ps[0], 8.0,
                              msk[:, r * 512:(r + 1) * 512], ALU.mult, ALU.add)
                        chunks.append((sfA, p2[:, 0:512]))
                    em.mm(ps[1], qT[ro:ro + 64, co * T + qc * 128: co * T + (qc + 1) * 128],
                          kT[ro:ro + 64, co * T + start + 512: co * T + start + 1024],
                          start=True, stop=True)
                    em.op(nc.vector, 'scalar_tensor_tensor', sfB, ps[1], 8.0,
                          msk[:, (4 + r) * 512:(5 + r) * 512], ALU.mult, ALU.add)
                    chunks.append((sfB, p2[:, 512:1024]))
                    softmax_rows(em, nc, st, chunks)
                    jjs = list(range(4, 8)) if blk == 0 else list(range(8))
                    for jj in jjs:
                        em.trans(PS['t'][:, 0:128], p2[:, jj * 128:(jj + 1) * 128], eye)
                        em.op(nc.scalar, 'copy', pT[:, jj * 128:(jj + 1) * 128],
                              PS['t'][:, 0:128])
                    em.mm_acc(ps[2][:, 0:64],
                              [(pT[:, jj * 128:(jj + 1) * 128],
                                vtok[:, (start // 128 + jj) * 512 + h * 64:
                                     (start // 128 + jj) * 512 + (h + 1) * 64])
                               for jj in jjs])
                    em.op(nc.scalar, 'copy', otok[:, qc * 512 + h * 64: qc * 512 + (h + 1) * 64],
                          ps[2][:, 0:64])
            al.free(qT, kT, vtok, sfA, sfB, p2, pT, st)
            wosb = al.sb([128, 4 * D], BF, "wosb")
            load_w(em, nc, al, A['wo'][l * 512:(l + 1) * 512, :], wosb, 512, D)
            oTt = al.sb([128, 4 * 128], BF, "oTt")
            ut = al.sb([128, D], F32, "ut")
            for t in range(NT):
                for k in range(4):
                    em.trans(PS['t'][:, 0:128], otok[:, t * 512 + k * 128: t * 512 + (k + 1) * 128], eye)
                    em.op(nc.scalar, 'copy', oTt[:, k * 128:(k + 1) * 128], PS['t'][:, 0:128])
                em.dma(ut, usc[xoff + t * 128: xoff + (t + 1) * 128, :])
                for n in range(2):
                    em.mm_acc(ps[n], [(oTt[:, k * 128:(k + 1) * 128],
                                       wosb[:, k * D + n * 512: k * D + (n + 1) * 512])
                                      for k in range(4)])
                    em.op(nc.vector, 'tensor_tensor', ut[:, n * 512:(n + 1) * 512],
                          ut[:, n * 512:(n + 1) * 512], ps[n], ALU.add)
                em.dma(usc[xoff + t * 128: xoff + (t + 1) * 128, :], ut)
            al.free(wosb, oTt, ut, otok)
            lgb = al.sb([128, D], BF, "lgb")
            lbb = al.sb([128, D], BF, "lbb")
            for nm, dst in (('ln_g', lgb), ('ln_b', lbb)):
                em.dma(row, A[nm][l:l + 1, :])
                for n in range(2):
                    bcast_row(em, nc, ps[0], ones1, row[:, n * 512:(n + 1) * 512], 512)
                    em.op(nc.scalar, 'copy', dst[:, n * 512:(n + 1) * 512], ps[0])
            b1ab = al.sb([128, HID], BF, "b1ab")
            b1gb = al.sb([128, HID], BF, "b1gb")
            rowh = al.sb([1, HID], F32, "rowh")
            for half, dst in ((0, b1ab), (1, b1gb)):
                em.dma(rowh, A['b1'][l:l + 1, half * HID:(half + 1) * HID])
                for n in range(6):
                    w = min(512, HID - n * 512)
                    bcast_row(em, nc, ps[0], ones1, rowh[:, n * 512:n * 512 + w], w)
                    em.op(nc.scalar, 'copy', dst[:, n * 512:n * 512 + w], ps[0][:, 0:w])
            al.free(rowh)
            b2b = al.sb([128, D], F32, "b2b")
            em.dma(row, A['b2'][l:l + 1, :])
            for n in range(2):
                bcast_row(em, nc, ps[0], ones1, row[:, n * 512:(n + 1) * 512], 512)
                em.op(nc.scalar, 'copy', b2b[:, n * 512:(n + 1) * 512], ps[0])
            NCH = [(0, 512), (512, 512), (1024, 512), (1536, 512), (2048, 512), (2560, 170)]
            xt = al.sb([128, D], F32, "xt")
            sh = al.sb([128, 512], F32, "sh")
            tmp = al.sb([128, D], F32, "tmp")
            st = al.sb([128, 8], F32, "st")
            hb = al.sb([128, D], BF, "hb")
            gf = al.sb([128, 512], F32, "gf")
            ldw = al.sb([128, HID], F32, "ldw")
            for hf in range(4):
                h2nT = al.sb([128, NK * 512], BF, "h2nT")
                for ti in range(4):
                    t = hf * 4 + ti
                    em.dma(xt, usc[xoff + t * 128: xoff + (t + 1) * 128, :])
                    if t == 0:
                        em.op(nc.vector, 'memset', sh[0:1, :], 0.0)
                        em.dma(sh[1:128, :], usc[xoff: xoff + 127, 512:1024])
                    else:
                        em.dma(sh, usc[xoff + t * 128 - 1: xoff + (t + 1) * 128 - 1, 512:1024])
                    em.op(nc.vector, 'tensor_copy', tmp[:, 0:512], xt[:, 0:512])
                    em.op(nc.vector, 'tensor_copy', tmp[:, 512:1024], sh)
                    ln_tile(em, nc, tmp, tmp, xt, st, D, eps)
                    em.op(nc.vector, 'tensor_tensor', tmp, tmp, lgb, ALU.mult)
                    em.op(nc.vector, 'tensor_tensor', hb, tmp, lbb, ALU.add)
                    for k in range(NK):
                        em.trans(PS['t'][:, 0:128], hb[:, k * 128:(k + 1) * 128], eye)
                        em.op(nc.scalar, 'copy', h2nT[:, k * 512 + ti * 128: k * 512 + (ti + 1) * 128],
                              PS['t'][:, 0:128])
                aH = al.sb([128, 4 * HID], BF, "aH")
                w1sb = al.sb([128, NK * HID], BF, "w1sb")
                for j in range(NK):
                    em.dma(ldw, A['w1'][l * D + j * 128: l * D + (j + 1) * 128, 0:HID])
                    em.op(nc.vector, 'tensor_copy', w1sb[:, j * HID:(j + 1) * HID], ldw)
                for ti in range(4):
                    for n0, w in NCH:
                        em.mm_acc(ps[0][:, 0:w],
                                  [(h2nT[:, k * 512 + ti * 128: k * 512 + (ti + 1) * 128],
                                    w1sb[:, k * HID + n0: k * HID + n0 + w])
                                   for k in range(NK)])
                        em.op(nc.vector, 'tensor_tensor', aH[:, ti * HID + n0: ti * HID + n0 + w],
                              ps[0][:, 0:w], b1ab[:, n0:n0 + w], ALU.add)
                for j in range(NK):
                    em.dma(ldw, A['w1'][l * D + j * 128: l * D + (j + 1) * 128, HID:2 * HID])
                    em.op(nc.vector, 'tensor_copy', w1sb[:, j * HID:(j + 1) * HID], ldw)
                actb = al.sb([128, 2816], BF, "actb")
                actT = al.sb([128, 2816], BF, "actT")
                wt2 = al.sb([128, D], BF, "wt2")
                ld2 = al.sb([128, D], F32, "ld2")
                ut = al.sb([128, D], F32, "ut")
                for ti in range(4):
                    t = hf * 4 + ti
                    for n0, w in NCH:
                        em.mm_acc(ps[0][:, 0:w],
                                  [(h2nT[:, k * 512 + ti * 128: k * 512 + (ti + 1) * 128],
                                    w1sb[:, k * HID + n0: k * HID + n0 + w])
                                   for k in range(NK)])
                        em.op(nc.vector, 'tensor_tensor', gf[:, 0:w], ps[0][:, 0:w],
                              b1gb[:, n0:n0 + w], ALU.add)
                        em.op(nc.scalar, 'activation', gf[:, 0:w], gf[:, 0:w], AF.Gelu)
                        em.op(nc.vector, 'tensor_tensor', actb[:, n0:n0 + w],
                              aH[:, ti * HID + n0: ti * HID + n0 + w], gf[:, 0:w], ALU.mult)
                    for k in range(22):
                        w = min(128, HID - k * 128)
                        em.trans(PS['t'][0:w, 0:128], actb[:, k * 128: k * 128 + w], eye)
                        em.op(nc.scalar, 'copy', actT[0:w, k * 128:(k + 1) * 128],
                              PS['t'][0:w, 0:128])
                    em.dma(ut, usc[xoff + t * 128: xoff + (t + 1) * 128, :])
                    for k in range(22):
                        w = min(128, HID - k * 128)
                        em.dma(ld2[0:w, :], A['w2'][l * HID + k * 128: l * HID + k * 128 + w, :])
                        em.op(nc.vector, 'tensor_copy', wt2[0:w, :], ld2[0:w, :])
                        for n in range(2):
                            em.mm(ps[1 + n][:, 0:512], actT[0:w, k * 128:(k + 1) * 128],
                                  wt2[0:w, n * 512:(n + 1) * 512],
                                  start=(k == 0), stop=(k == 21), sgc=True)
                    for n in range(2):
                        em.op(nc.vector, 'tensor_tensor', ut[:, n * 512:(n + 1) * 512],
                              ut[:, n * 512:(n + 1) * 512], ps[1 + n], ALU.add)
                        em.op(nc.vector, 'tensor_tensor', ut[:, n * 512:(n + 1) * 512],
                              ut[:, n * 512:(n + 1) * 512], b2b[:, n * 512:(n + 1) * 512],
                              ALU.add)
                    em.dma(usc[xoff + t * 128: xoff + (t + 1) * 128, :], ut)
                al.free(h2nT, aH, w1sb, actb, actT, wt2, ld2, ut)
            al.free(lgb, lbb, b1ab, b1gb, b2b, xt, sh, tmp, st, hb, gf, ldw)
        xt = al.sb([128, D], F32, "xt")
        ut = al.sb([128, D], F32, "ut")
        for t in range(NT):
            em.dma(xt, x_ap[xoff + t * 128: xoff + (t + 1) * 128, :])
            em.dma(ut, usc[xoff + t * 128: xoff + (t + 1) * 128, :])
            em.op(nc.vector, 'tensor_tensor', ut, ut, xt, ALU.add)
            em.dma(y_ap[xoff + t * 128: xoff + (t + 1) * 128, :], ut)
        al.free(xt, ut)
    al.free(eye, eps, ones1, row, msk)


def s2_tables():
    n, DIM_HEAD, WINDOW = T, 64, 512
    half_idx = np.arange(0, DIM_HEAD, 2, dtype=np.float64)
    inv_freq = 1.0 / 10000.0 ** (half_idx / DIM_HEAD)
    t = np.arange(n, dtype=np.float64)
    freqs = t[:, None] * inv_freq[None, :]
    freqs = np.concatenate([freqs, freqs], axis=-1)
    base = (half_idx + 0.4 * DIM_HEAD) / (1.4 * DIM_HEAD)
    power = (t - n // 2) / float(WINDOW // 2)
    scale = base[None, :] ** power[:, None]
    scale = np.concatenate([scale, scale], axis=-1)
    cos, sin = np.cos(freqs), np.sin(freqs)
    aq = np.tile(cos * scale, (1, 8)).astype(np.float32)
    bq = np.tile(sin * scale, (1, 8)).astype(np.float32)
    ak = np.tile(cos / scale, (1, 8)).astype(np.float32)
    bk = np.tile(sin / scale, (1, 8)).astype(np.float32)
    il = np.arange(128)[:, None]
    jl = np.arange(512)[None, :]
    m = np.zeros((8, 128, 512), np.float32)
    for r in range(4):
        m[r][jl < il + 128 * r] = -1e30
        m[4 + r][jl > il + 128 * r] = -1e30
    return aq, bq, ak, bk, m.reshape(8 * 128, 512)


S2_TENSORS = [('x', (BPC * T, D), 'ExternalInput'),
              ('y', (BPC * T, D), 'ExternalOutput'),
              ('usc', (BPC * T, D), 'Internal'),
              ('wqkv', (2 * D, 1536), 'ExternalInput'),
              ('qsr', (2, 512), 'ExternalInput'),
              ('ksr', (2, 512), 'ExternalInput'),
              ('wo', (2 * 512, D), 'ExternalInput'),
              ('ln_g', (2, D), 'ExternalInput'),
              ('ln_b', (2, D), 'ExternalInput'),
              ('w1', (2 * D, 5460), 'ExternalInput'),
              ('b1', (2, 5460), 'ExternalInput'),
              ('w2', (2 * 2730, D), 'ExternalInput'),
              ('b2', (2, D), 'ExternalInput'),
              ('aq', (T, 512), 'ExternalInput'),
              ('bq', (T, 512), 'ExternalInput'),
              ('ak', (T, 512), 'ExternalInput'),
              ('bk', (T, 512), 'ExternalInput'),
              ('m2', (8 * 128, 512), 'ExternalInput'),
              ('eye', (128, 128), 'ExternalInput')]


def build_s6(nc, em, al, PS, A):
    H, HD, E = 16, 64, 8
    x_ap, y_ap, x1sc = A['x'], A['y'], A['x1sc']
    eye = load_eye(em, nc, al, A)
    eps = al.sb([128, 1], F32, "eps")
    em.op(nc.vector, 'memset', eps, 1e-5)
    ones1 = al.sb([1, 128], F32, "ones1")
    em.op(nc.vector, 'memset', ones1, 1.0)
    ps = PS['f']
    row = al.sb([1, D], F32, "row")

    def mk_bcast(name, N=D):
        dst = al.sb([128, N], F32, name)
        em.dma(row[:, 0:N], A[name][0:1, 0:N])
        for n in range((N + 511) // 512):
            w = min(512, N - n * 512)
            bcast_row(em, nc, ps[0], ones1, row[:, n * 512:n * 512 + w], w)
            em.op(nc.scalar, 'copy', dst[:, n * 512:n * 512 + w], ps[0][:, 0:w])
        return dst

    b1c = al.sb([128, E * 32], F32, "b1c")
    for e in range(E):
        for f in range(32):
            em.dma(b1c[:, e * 32 + f: e * 32 + f + 1],
                   A['b1'][e * 4096 + f * 128: e * 4096 + (f + 1) * 128, :])

    for b in range(BPC):
        xoff = b * T
        ct = al.sb([128, 512], F32, "ct6")
        st6 = al.sb([128, 512], F32, "st6")
        swp = al.sb([128, 512], F32, "swp6")

        def rope6(q_f32, t, grp, ct=ct, st6=st6, swp=swp):
            qv = q_f32.rearrange("p (h s) -> p h s", s=64)
            sv = swp.rearrange("p (h s) -> p h s", s=64)
            em.op(nc.vector, 'tensor_scalar_mul', sv[:, :, 0:32], qv[:, :, 32:64], -1.0)
            em.op(nc.vector, 'tensor_copy', sv[:, :, 32:64], qv[:, :, 0:32])
            em.op(nc.vector, 'tensor_tensor', q_f32, q_f32, ct, ALU.mult)
            em.op(nc.vector, 'tensor_tensor', swp, swp, st6, ALU.mult)
            em.op(nc.vector, 'tensor_tensor', q_f32, q_f32, swp, ALU.add)

        zT = al.sb([128, NK * T], BF, "zT")
        build_zT(em, nc, al, PS, x_ap, xoff, eye, zT, eps)
        aoT = al.sb([128, NK * T], BF, "aoT")

        for grp in range(2):
            qbb = mk_bcast('bq')
            kbb = mk_bcast('bk')
            wq = al.sb([128, NK * D], BF, "wq")
            wk = al.sb([128, NK * D], BF, "wk")
            load_w(em, nc, al, A['Wq'], wq, D, D)
            load_w(em, nc, al, A['Wk'], wk, D, D)
            qtm = al.sb([128, NT * 512], BF, "qtm")
            ktm = al.sb([128, NT * 512], BF, "ktm")
            qf = al.sb([128, 512], F32, "qf6")
            for t in range(NT):
                em.dma(ct, A['c6'][t * 128:(t + 1) * 128, grp * 512:(grp + 1) * 512])
                em.dma(st6, A['s6'][t * 128:(t + 1) * 128, grp * 512:(grp + 1) * 512])
                for wsb, bbt, dst in ((wq, qbb, qtm), (wk, kbb, ktm)):
                    em.mm_acc(ps[0],
                              [(zT[:, k * T + t * 128: k * T + (t + 1) * 128],
                                wsb[:, k * D + grp * 512: k * D + (grp + 1) * 512])
                               for k in range(NK)])
                    em.op(nc.vector, 'tensor_tensor', qf, ps[0],
                          bbt[:, grp * 512:(grp + 1) * 512], ALU.add)
                    rope6(qf, t, grp)
                    em.op(nc.vector, 'tensor_copy', dst[:, t * 512:(t + 1) * 512], qf)
            al.free(wq, wk, qbb, kbb, qf)
            QT = al.sb([128, 4 * T], BF, "QT6")
            KT = al.sb([128, 4 * T], BF, "KT6")
            for src, dst in ((qtm, QT), (ktm, KT)):
                for t in range(NT):
                    for k in range(4):
                        em.trans(PS['t'][:, 0:128],
                                 src[:, t * 512 + k * 128: t * 512 + (k + 1) * 128], eye)
                        em.op(nc.scalar, 'copy',
                              dst[:, k * T + t * 128: k * T + (t + 1) * 128],
                              PS['t'][:, 0:128])
            al.free(qtm, ktm)
            vbb = mk_bcast('bv')
            wv = al.sb([128, NK * D], BF, "wv")
            load_w(em, nc, al, A['Wv'], wv, D, D)
            Vg = al.sb([128, NT * 8 * 65], BF, "Vg6")
            for t in range(NT):
                base = t * 8 * 65
                em.mm_acc(ps[0],
                          [(zT[:, k * T + t * 128: k * T + (t + 1) * 128],
                            wv[:, k * D + grp * 512: k * D + (grp + 1) * 512])
                           for k in range(NK)])
                for hh in range(8):
                    em.op(nc.vector, 'tensor_tensor',
                          Vg[:, base + hh * 65: base + hh * 65 + 64],
                          ps[0][:, hh * 64:(hh + 1) * 64],
                          vbb[:, grp * 512 + hh * 64: grp * 512 + (hh + 1) * 64], ALU.add)
                em.op(nc.vector, 'memset',
                      Vg[:, base:base + 8 * 65].rearrange("p (h c) -> p h c", c=65)[:, :, 64:65],
                      1.0)
            al.free(wv, vbb)
            expT = al.sb([128, NT * 512], BF, "expT6")
            dsb = al.sb([64, 512], F32, "dsb6")
            drow = al.sb([1, 512], F32, "drow6")
            for hh in range(8):
                h = grp * 8 + hh
                co, ro = hh // 2, (hh % 2) * 64
                for ic in range(4):
                    for jc in range(NT):
                        em.mm(ps[0],
                              KT[ro:ro + 64, co * T + jc * 128: co * T + (jc + 1) * 128],
                              QT[ro:ro + 64, co * T + ic * 512: co * T + (ic + 1) * 512],
                              start=True, stop=True)
                        em.op(nc.scalar, 'activation', expT[:, jc * 512:(jc + 1) * 512],
                              ps[0], AF.Exp, scale=0.125)
                    em.mm_acc(ps[1][0:65, :],
                              [(Vg[:, jc * 8 * 65 + hh * 65: jc * 8 * 65 + hh * 65 + 65],
                                expT[:, jc * 512:(jc + 1) * 512]) for jc in range(NT)])
                    em.op(nc.vector, 'reciprocal', drow, ps[1][64:65, :])
                    bcast_row(em, nc, ps[2][0:64, :], ones1[0:1, 0:64], drow, 512)
                    em.op(nc.scalar, 'copy', dsb, ps[2][0:64, :])
                    em.op(nc.vector, 'tensor_tensor',
                          aoT[ro:ro + 64, (h // 2) * T + ic * 512:(h // 2) * T + (ic + 1) * 512],
                          ps[1][0:64, :], dsb, ALU.mult)
            al.free(QT, KT, Vg, expT, dsb, drow)
        al.free(zT, ct, st6, swp)

        bob = mk_bcast('bo')
        g1b = mk_bcast('g1')
        be1b = mk_bcast('be1')
        bgb = mk_bcast('bg', 8)
        wo = al.sb([128, NK * D], BF, "wo6")
        load_w(em, nc, al, A['Wo'], wo, D, D)
        wg = al.sb([128, NK * 8], BF, "wg6")
        load_w(em, nc, al, A['Wg'], wg, D, 8)
        x1T = al.sb([128, NK * T], BF, "x1T")
        comb = al.sb([128, NT * 8], F32, "comb")
        xt = al.sb([128, D], F32, "xt6")
        sfull = al.sb([128, D], F32, "sf6")
        tmp = al.sb([128, D], F32, "tmp6")
        st = al.sb([128, 8], F32, "st6g")
        x1b = al.sb([128, D], BF, "x1b")
        gp = al.sb([128, 8], F32, "gp")
        gm = al.sb([128, 8], F32, "gm")
        for t in range(NT):
            em.dma(xt, x_ap[xoff + t * 128: xoff + (t + 1) * 128, :])
            ln_tile(em, nc, xt, xt, tmp, st, D, eps)
            for n in range(2):
                em.mm_acc(ps[n],
                          [(aoT[:, k * T + t * 128: k * T + (t + 1) * 128],
                            wo[:, k * D + n * 512: k * D + (n + 1) * 512])
                           for k in range(NK)])
                em.op(nc.vector, 'tensor_tensor', sfull[:, n * 512:(n + 1) * 512],
                      ps[n], bob[:, n * 512:(n + 1) * 512], ALU.add)
            em.op(nc.vector, 'tensor_tensor', sfull, sfull, xt, ALU.add)
            ln_tile(em, nc, sfull, sfull, tmp, st, D, eps)
            em.op(nc.vector, 'tensor_tensor', sfull, sfull, g1b, ALU.mult)
            em.op(nc.vector, 'tensor_tensor', sfull, sfull, be1b, ALU.add)
            em.dma(x1sc[xoff + t * 128: xoff + (t + 1) * 128, :], sfull)
            em.op(nc.vector, 'tensor_copy', x1b, sfull)
            for k in range(NK):
                em.trans(PS['t'][:, 0:128], x1b[:, k * 128:(k + 1) * 128], eye)
                em.op(nc.scalar, 'copy', x1T[:, k * T + t * 128: k * T + (t + 1) * 128],
                      PS['t'][:, 0:128])
            em.mm_acc(ps[0][:, 0:8],
                      [(x1T[:, k * T + t * 128: k * T + (t + 1) * 128],
                        wg[:, k * 8:(k + 1) * 8]) for k in range(NK)])
            em.op(nc.vector, 'tensor_tensor', gp, ps[0][:, 0:8], bgb, ALU.add)
            softmax_rows(em, nc, st, [(gp, gp)])
            em.op(nc.vector, 'reduce_max', st[:, 4:5], gp, axis=AX.X)
            em.op(nc.vector, 'tensor_scalar', gm, gp, st[:, 4:5], None, ALU.is_equal)
            em.op(nc.vector, 'scalar_tensor_tensor', tmp[:, 0:8], gm, -2.0, gp,
                  ALU.mult, ALU.add)
            em.op(nc.vector, 'reduce_max', st[:, 5:6], tmp[:, 0:8], axis=AX.X)
            em.op(nc.vector, 'tensor_scalar', tmp[:, 0:8], tmp[:, 0:8], st[:, 5:6],
                  None, ALU.is_equal)
            em.op(nc.vector, 'tensor_tensor', gm, gm, tmp[:, 0:8], ALU.add)
            em.op(nc.vector, 'tensor_tensor', st[:, 6:7], st[:, 4:5], st[:, 5:6], ALU.add)
            em.op(nc.vector, 'reciprocal', st[:, 6:7], st[:, 6:7])
            em.op(nc.vector, 'tensor_tensor', gm, gm, gp, ALU.mult)
            em.op(nc.vector, 'tensor_scalar_mul', comb[:, t * 8:(t + 1) * 8], gm,
                  st[:, 6:7])
        al.free(aoT, bob, g1b, be1b, bgb, wo, wg, xt, sfull, tmp, st, x1b, gp, gm)

        g2b = mk_bcast('g2')
        be2b = mk_bcast('be2')
        b2b = al.sb([128, D], F32, "b2b")
        wbuf = al.sb([128, 32768], BF, "wbuf")
        hT = al.sb([128, 32 * 512], BF, "hT6")
        ld = al.sb([128, 1024], F32, "ld6")
        tmpf = al.sb([128, 512], F32, "tpf6")
        acc = al.sb([128, 4 * D], F32, "acc6")
        xt = al.sb([128, D], F32, "xt6b")
        tmp = ld
        st = al.sb([128, 8], F32, "st6b")
        yt = al.sb([128, D], F32, "yt6")
        for g in range(4):
            em.op(nc.vector, 'memset', acc, 0.0)
            for e in range(E):
                for j in range(NK):
                    for q4 in range(4):
                        em.dma(ld, A['W1'][e * 1024 + j * 128: e * 1024 + (j + 1) * 128,
                                           q4 * 1024:(q4 + 1) * 1024])
                        em.op(nc.vector, 'tensor_copy',
                              wbuf[:, j * 4096 + q4 * 1024: j * 4096 + (q4 + 1) * 1024],
                              ld)
                for f in range(32):
                    em.mm_acc(ps[0],
                              [(wbuf[:, k * 4096 + f * 128: k * 4096 + (f + 1) * 128],
                                x1T[:, k * T + g * 512: k * T + (g + 1) * 512])
                               for k in range(NK)])
                    em.op(nc.scalar, 'activation', hT[:, f * 512:(f + 1) * 512],
                          ps[0], AF.Gelu, bias=b1c[:, e * 32 + f: e * 32 + f + 1],
                          scale=1.0)
                for f in range(32):
                    em.dma(ld[:, 0:1024],
                           A['W2'][e * 4096 + f * 128: e * 4096 + (f + 1) * 128, :])
                    em.op(nc.vector, 'tensor_copy',
                          wbuf[:, f * 1024:(f + 1) * 1024], ld[:, 0:1024])
                em.dma(row, A['b2'][e:e + 1, :])
                for n in range(2):
                    bcast_row(em, nc, ps[0], ones1, row[:, n * 512:(n + 1) * 512], 512)
                    em.op(nc.scalar, 'copy', b2b[:, n * 512:(n + 1) * 512], ps[0])
                for tt in range(4):
                    tg = g * 4 + tt
                    for n in range(2):
                        em.mm_acc(ps[1],
                                  [(hT[:, f * 512 + tt * 128: f * 512 + (tt + 1) * 128],
                                    wbuf[:, f * 1024 + n * 512: f * 1024 + (n + 1) * 512])
                                   for f in range(32)])
                        em.op(nc.vector, 'tensor_tensor', tmpf, ps[1],
                              b2b[:, n * 512:(n + 1) * 512], ALU.add)
                        em.op(nc.vector, 'scalar_tensor_tensor',
                              acc[:, tt * D + n * 512: tt * D + (n + 1) * 512],
                              tmpf, comb[:, tg * 8 + e: tg * 8 + e + 1],
                              acc[:, tt * D + n * 512: tt * D + (n + 1) * 512],
                              ALU.mult, ALU.add)
            for tt in range(4):
                tg = g * 4 + tt
                em.dma(xt, x1sc[xoff + tg * 128: xoff + (tg + 1) * 128, :])
                em.op(nc.vector, 'tensor_tensor', yt, xt, acc[:, tt * D:(tt + 1) * D],
                      ALU.add)
                ln_tile(em, nc, yt, yt, tmp, st, D, eps)
                em.op(nc.vector, 'tensor_tensor', yt, yt, g2b, ALU.mult)
                em.op(nc.vector, 'tensor_tensor', yt, yt, be2b, ALU.add)
                em.dma(xt, x_ap[xoff + tg * 128: xoff + (tg + 1) * 128, :])
                em.op(nc.vector, 'tensor_tensor', yt, yt, xt, ALU.add)
                em.dma(y_ap[xoff + tg * 128: xoff + (tg + 1) * 128, :], yt)
        al.free(g2b, be2b, b2b, wbuf, hT, ld, tmpf, acc, xt, st, yt, x1T, comb)
    al.free(eye, eps, ones1, row, b1c)


def s6_tables():
    inv = 1.0 / 10000.0 ** (np.arange(0, 64, 2, dtype=np.float64) / 64)
    t = np.arange(T, dtype=np.float64)
    freqs = t[:, None] * inv[None, :]
    emb = np.concatenate([freqs, freqs], axis=-1)
    c = np.tile(np.cos(emb), (1, 16)).astype(np.float32)
    s = np.tile(np.sin(emb), (1, 16)).astype(np.float32)
    return c, s


S6_TENSORS = [('x', (BPC * T, D), 'ExternalInput'),
              ('y', (BPC * T, D), 'ExternalOutput'),
              ('x1sc', (BPC * T, D), 'Internal'),
              ('Wq', (D, D), 'ExternalInput'),
              ('Wk', (D, D), 'ExternalInput'),
              ('Wv', (D, D), 'ExternalInput'),
              ('Wo', (D, D), 'ExternalInput'),
              ('bq', (1, D), 'ExternalInput'),
              ('bk', (1, D), 'ExternalInput'),
              ('bv', (1, D), 'ExternalInput'),
              ('bo', (1, D), 'ExternalInput'),
              ('Wg', (D, 8), 'ExternalInput'),
              ('bg', (1, 8), 'ExternalInput'),
              ('W1', (8 * 1024, 4096), 'ExternalInput'),
              ('b1', (8 * 4096, 1), 'ExternalInput'),
              ('W2', (8 * 4096, 1024), 'ExternalInput'),
              ('b2', (8, 1024), 'ExternalInput'),
              ('g1', (1, D), 'ExternalInput'),
              ('be1', (1, D), 'ExternalInput'),
              ('g2', (1, D), 'ExternalInput'),
              ('be2', (1, D), 'ExternalInput'),
              ('c6', (T, D), 'ExternalInput'),
              ('s6', (T, D), 'ExternalInput'),
              ('eye', (128, 128), 'ExternalInput')]


def build_program(tensors, builder, nf=4096, nb=89856):
    nc = bass.Bass("TRN2", target_bir_lowering=False)
    A = {}
    for name, shape, kind in tensors:
        A[name] = nc.dram_tensor(name, list(shape), F32, kind=kind).ap()
    with (
        nc.sbuf_tensor("poolf", [128, nf], F32) as pf,
        nc.sbuf_tensor("poolb", [128, nb], BF) as pb,
        nc.psum_tensor("ps0", [128, 512], F32) as ps0,
        nc.psum_tensor("ps1", [128, 512], F32) as ps1,
        nc.psum_tensor("ps2", [128, 512], F32) as ps2,
        nc.psum_tensor("ps3", [128, 512], F32) as ps3,
        nc.psum_tensor("ps4", [128, 512], F32) as ps4,
        nc.psum_tensor("ps5", [128, 512], F32) as ps5,
        nc.psum_tensor("pt0", [128, 1024], BF) as pt0,
        nc.semaphore("g") as sem,
    ):
        em = Em(nc, sem)
        al = Al(pf[:, :], pb[:, :], nf, nb)
        PS = {'f': [ps0[:, :], ps1[:, :], ps2[:, :], ps3[:, :], ps4[:, :], ps5[:, :]],
              't': pt0[:, :]}
        builder(nc, em, al, PS, A)
        em.done()
    return nc


S1_TENSORS = [('x', (BPC * T, D), 'ExternalInput'),
              ('y', (BPC * T, D), 'ExternalOutput'),
              ('Wq', (D, D), 'ExternalInput'),
              ('Wk', (D, D), 'ExternalInput'),
              ('Wv', (D, D), 'ExternalInput'),
              ('Wo', (D, D), 'ExternalInput'),
              ('eye', (128, 128), 'ExternalInput')]


def kernel(**inputs):
    ii = inputs
    eye = np.eye(128, dtype=np.float32)
    xs = np.ascontiguousarray(np.asarray(ii['L0_attn_a_x']).reshape(16 * T, D))
    shards = [np.ascontiguousarray(s) for s in np.split(xs, 8, axis=0)]

    nc1 = build_program(S1_TENSORS, build_s1)
    w1 = {'Wq': ii['L0_attn_a_Wq'], 'Wk': ii['L0_attn_a_Wk'],
          'Wv': ii['L0_attn_a_Wv'], 'Wo': ii['L0_attn_a_Wo'], 'eye': eye}

    aq, bq, ak, bk, m2 = s2_tables()
    nc2 = build_program(S2_TENSORS, build_s2, nf=16384, nb=65408)
    w2 = {'wqkv': np.ascontiguousarray(ii['L0_ffn_wqkv'].reshape(2 * 1024, 1536)),
          'qsr': np.ascontiguousarray(np.tile(ii['L0_ffn_q_scale'], (1, 8))),
          'ksr': np.ascontiguousarray(np.tile(ii['L0_ffn_k_scale'], (1, 8))),
          'wo': np.ascontiguousarray(ii['L0_ffn_wo'].reshape(2 * 512, 1024)),
          'ln_g': ii['L0_ffn_ln_g'], 'ln_b': ii['L0_ffn_ln_b'],
          'w1': np.ascontiguousarray(ii['L0_ffn_w1'].reshape(2 * 1024, 5460)),
          'b1': ii['L0_ffn_b1'],
          'w2': np.ascontiguousarray(ii['L0_ffn_w2'].reshape(2 * 2730, 1024)),
          'b2': ii['L0_ffn_b2'],
          'aq': aq, 'bq': bq, 'ak': ak, 'bk': bk, 'm2': m2, 'eye': eye}

    nc3 = build_program(S3_TENSORS, build_s3, nf=8192, nb=81792)
    w3 = {'q_w': ii['L0_attn_b_q_w'], 'k_w': ii['L0_attn_b_k_w'],
          'v_w': ii['L0_attn_b_v_w'], 'o_w': ii['L0_attn_b_o_w'],
          'q_b': ii['L0_attn_b_q_b'].reshape(D, 1),
          'k_b': ii['L0_attn_b_k_b'].reshape(D, 1),
          'v_b': ii['L0_attn_b_v_b'].reshape(1, D),
          'o_b': ii['L0_attn_b_o_b'].reshape(1, D),
          'm3': s3_masks(), 'eye': eye}

    nc5 = build_program(S5_TENSORS, build_s5, nf=6144, nb=85888)
    w5 = {'Wq': ii['L1_attn_a_Wq'], 'Wk': ii['L1_attn_a_Wk'],
          'Wv': ii['L1_attn_a_Wv'], 'Wo': ii['L1_attn_a_Wo'],
          'bq': ii['L1_attn_a_bq'].reshape(D, 1),
          'bk': ii['L1_attn_a_bk'].reshape(D, 1),
          'bv': ii['L1_attn_a_bv'].reshape(1, D),
          'bo': ii['L1_attn_a_bo'].reshape(1, D),
          'proj': ii['L1_attn_a_proj'], 'm5': s5_mask(), 'eye': eye}

    c6, s6t = s6_tables()
    nc6 = build_program(S6_TENSORS, build_s6, nf=12800, nb=72576)
    w6 = {'Wq': ii['L1_ffn_Wq'], 'Wk': ii['L1_ffn_Wk'], 'Wv': ii['L1_ffn_Wv'],
          'Wo': ii['L1_ffn_Wo'], 'Wg': ii['L1_ffn_Wg'],
          'bq': ii['L1_ffn_bq'].reshape(1, D), 'bk': ii['L1_ffn_bk'].reshape(1, D),
          'bv': ii['L1_ffn_bv'].reshape(1, D), 'bo': ii['L1_ffn_bo'].reshape(1, D),
          'bg': ii['L1_ffn_bg'].reshape(1, 8),
          'W1': np.ascontiguousarray(ii['L1_ffn_W1'].reshape(8 * 1024, 4096)),
          'b1': np.ascontiguousarray(ii['L1_ffn_b1'].reshape(8 * 4096, 1)),
          'W2': np.ascontiguousarray(ii['L1_ffn_W2'].reshape(8 * 4096, 1024)),
          'b2': ii['L1_ffn_b2'],
          'g1': ii['L1_ffn_g1'].reshape(1, D), 'be1': ii['L1_ffn_be1'].reshape(1, D),
          'g2': ii['L1_ffn_g2'].reshape(1, D), 'be2': ii['L1_ffn_be2'].reshape(1, D),
          'c6': c6, 's6': s6t, 'eye': eye}

    ct7, st7, m7 = s7_tables()
    nc7 = build_program(S7_TENSORS, build_s7, nf=6144, nb=85888)
    w7 = {'W_DQ': ii['L1_attn_b_W_DQ'], 'W_UQ': ii['L1_attn_b_W_UQ'],
          'W_QR': ii['L1_attn_b_W_QR'], 'W_DKV': ii['L1_attn_b_W_DKV'],
          'W_UK': ii['L1_attn_b_W_UK'], 'W_UV': ii['L1_attn_b_W_UV'],
          'W_KR': ii['L1_attn_b_W_KR'], 'W_O': ii['L1_attn_b_W_O'],
          'ct7': ct7, 'st7t': st7, 'm7': m7, 'eye': eye}

    seq = [(nc1, w1), (nc2, w2), (nc3, w3), (nc2, w2),
           (nc5, w5), (nc6, w6), (nc7, w7), (nc6, w6)]
    cur = shards
    for nc, wmap in seq:
        res = run_bass_kernel_spmd(nc, [dict(x=c, **wmap) for c in cur],
                                   core_ids=list(range(8)))
        cur = [np.ascontiguousarray(np.asarray(r['y'])) for r in res.results]
    return np.concatenate(cur, axis=0).reshape(16, T, D)


def build_fused(specs):
    nc = bass.Bass("TRN2", target_bir_lowering=False)
    declared = {}

    def decl(name, shape, kind):
        if name not in declared:
            declared[name] = nc.dram_tensor(name, list(shape), F32, kind=kind).ap()
        return declared[name]

    for builder, tensors, prefix, amap, nf, nb in specs:
        for lname, shape, kind in tensors:
            gname = amap.get(lname, prefix + lname)
            decl(gname, shape, kind)
    with (
        nc.psum_tensor("ps0", [128, 512], F32) as ps0,
        nc.psum_tensor("ps1", [128, 512], F32) as ps1,
        nc.psum_tensor("ps2", [128, 512], F32) as ps2,
        nc.psum_tensor("ps3", [128, 512], F32) as ps3,
        nc.psum_tensor("ps4", [128, 512], F32) as ps4,
        nc.psum_tensor("ps5", [128, 512], F32) as ps5,
        nc.psum_tensor("pt0", [128, 1024], BF) as pt0,
        nc.semaphore("g") as sem,
    ):
        em = Em(nc, sem)
        PS = {'f': [ps0[:, :], ps1[:, :], ps2[:, :], ps3[:, :], ps4[:, :], ps5[:, :]],
              't': pt0[:, :]}
        for i, (builder, tensors, prefix, amap, nf, nb) in enumerate(specs):
            A = {}
            for lname, shape, kind in tensors:
                A[lname] = declared[amap.get(lname, prefix + lname)]
            with (
                nc.sbuf_tensor(f"pf{i}", [128, nf], F32) as pf,
                nc.sbuf_tensor(f"pb{i}", [128, nb], BF) as pb,
            ):
                al = Al(pf[:, :], pb[:, :], nf, nb)
                builder(nc, em, al, PS, A)
        em.done()
    return nc


def _wmaps(ii):
    eye = np.eye(128, dtype=np.float32)
    w1 = {'Wq': ii['L0_attn_a_Wq'], 'Wk': ii['L0_attn_a_Wk'],
          'Wv': ii['L0_attn_a_Wv'], 'Wo': ii['L0_attn_a_Wo'], 'eye': eye}
    aq, bq, ak, bk, m2 = s2_tables()
    w2 = {'wqkv': np.ascontiguousarray(ii['L0_ffn_wqkv'].reshape(2 * 1024, 1536)),
          'qsr': np.ascontiguousarray(np.tile(ii['L0_ffn_q_scale'], (1, 8))),
          'ksr': np.ascontiguousarray(np.tile(ii['L0_ffn_k_scale'], (1, 8))),
          'wo': np.ascontiguousarray(ii['L0_ffn_wo'].reshape(2 * 512, 1024)),
          'ln_g': ii['L0_ffn_ln_g'], 'ln_b': ii['L0_ffn_ln_b'],
          'w1': np.ascontiguousarray(ii['L0_ffn_w1'].reshape(2 * 1024, 5460)),
          'b1': ii['L0_ffn_b1'],
          'w2': np.ascontiguousarray(ii['L0_ffn_w2'].reshape(2 * 2730, 1024)),
          'b2': ii['L0_ffn_b2'],
          'aq': aq, 'bq': bq, 'ak': ak, 'bk': bk, 'm2': m2, 'eye': eye}
    w3 = {'q_w': ii['L0_attn_b_q_w'], 'k_w': ii['L0_attn_b_k_w'],
          'v_w': ii['L0_attn_b_v_w'], 'o_w': ii['L0_attn_b_o_w'],
          'q_b': ii['L0_attn_b_q_b'].reshape(D, 1),
          'k_b': ii['L0_attn_b_k_b'].reshape(D, 1),
          'v_b': ii['L0_attn_b_v_b'].reshape(1, D),
          'o_b': ii['L0_attn_b_o_b'].reshape(1, D),
          'm3': s3_masks(), 'eye': eye}
    w5 = {'Wq': ii['L1_attn_a_Wq'], 'Wk': ii['L1_attn_a_Wk'],
          'Wv': ii['L1_attn_a_Wv'], 'Wo': ii['L1_attn_a_Wo'],
          'bq': ii['L1_attn_a_bq'].reshape(D, 1),
          'bk': ii['L1_attn_a_bk'].reshape(D, 1),
          'bv': ii['L1_attn_a_bv'].reshape(1, D),
          'bo': ii['L1_attn_a_bo'].reshape(1, D),
          'proj': ii['L1_attn_a_proj'], 'm5': s5_mask(), 'eye': eye}
    c6, s6t = s6_tables()
    w6 = {'Wq': ii['L1_ffn_Wq'], 'Wk': ii['L1_ffn_Wk'], 'Wv': ii['L1_ffn_Wv'],
          'Wo': ii['L1_ffn_Wo'], 'Wg': ii['L1_ffn_Wg'],
          'bq': ii['L1_ffn_bq'].reshape(1, D), 'bk': ii['L1_ffn_bk'].reshape(1, D),
          'bv': ii['L1_ffn_bv'].reshape(1, D), 'bo': ii['L1_ffn_bo'].reshape(1, D),
          'bg': ii['L1_ffn_bg'].reshape(1, 8),
          'W1': np.ascontiguousarray(ii['L1_ffn_W1'].reshape(8 * 1024, 4096)),
          'b1': np.ascontiguousarray(ii['L1_ffn_b1'].reshape(8 * 4096, 1)),
          'W2': np.ascontiguousarray(ii['L1_ffn_W2'].reshape(8 * 4096, 1024)),
          'b2': ii['L1_ffn_b2'],
          'g1': ii['L1_ffn_g1'].reshape(1, D), 'be1': ii['L1_ffn_be1'].reshape(1, D),
          'g2': ii['L1_ffn_g2'].reshape(1, D), 'be2': ii['L1_ffn_be2'].reshape(1, D),
          'c6': c6, 's6': s6t, 'eye': eye}
    ct7, st7, m7 = s7_tables()
    w7 = {'W_DQ': ii['L1_attn_b_W_DQ'], 'W_UQ': ii['L1_attn_b_W_UQ'],
          'W_QR': ii['L1_attn_b_W_QR'], 'W_DKV': ii['L1_attn_b_W_DKV'],
          'W_UK': ii['L1_attn_b_W_UK'], 'W_UV': ii['L1_attn_b_W_UV'],
          'W_KR': ii['L1_attn_b_W_KR'], 'W_O': ii['L1_attn_b_W_O'],
          'ct7': ct7, 'st7t': st7, 'm7': m7, 'eye': eye}
    return w1, w2, w3, w5, w6, w7


FUSED_SPECS = None


def _fused_specs():
    res = [('r0', (BPC * T, D), 'Internal'), ('r1', (BPC * T, D), 'Internal'),
           ('scr', (BPC * T, D), 'Internal')]
    io = [('x', 'x'), ('r0', 'r0'), ('r1', 'r1'), ('scr', 'scr'), ('y', 'y')]
    specs = []
    chain = [
        (build_s1, S1_TENSORS, 's1_', 'x', 'r0', 4096, 89856),
        (build_s2, S2_TENSORS, 's2_', 'r0', 'r1', 16384, 65408),
        (build_s3, S3_TENSORS, 's3_', 'r1', 'r0', 8192, 81792),
        (build_s2, S2_TENSORS, 's2_', 'r0', 'r1', 16384, 65408),
        (build_s5, S5_TENSORS, 's5_', 'r1', 'r0', 6144, 85888),
        (build_s6, S6_TENSORS, 's6_', 'r0', 'r1', 12800, 72576),
        (build_s7, S7_TENSORS, 's7_', 'r1', 'r0', 6144, 85888),
        (build_s6, S6_TENSORS, 's6_', 'r0', 'y', 12800, 72576),
    ]
    for builder, tensors, prefix, xin, yout, nf, nb in chain:
        amap = {'x': xin, 'y': yout, 'usc': 'scr', 'osc': 'scr', 'x1sc': 'scr',
                'eye': 'eye'}
        tl = [(n, s, k) for (n, s, k) in tensors]
        fixed = []
        for n, s, k in tl:
            gn = amap.get(n, prefix + n)
            if gn in ('r0', 'r1', 'scr'):
                fixed.append((n, s, 'Internal'))
            elif gn == 'x':
                fixed.append((n, s, 'ExternalInput'))
            elif gn == 'y':
                fixed.append((n, s, 'ExternalOutput'))
            else:
                fixed.append((n, s, k))
        specs.append((builder, fixed, prefix, amap, nf, nb))
    return specs


def kernel_fused(**inputs):
    ii = inputs
    xs = np.ascontiguousarray(np.asarray(ii['L0_attn_a_x']).reshape(16 * T, D))
    shards = [np.ascontiguousarray(s) for s in np.split(xs, 8, axis=0)]
    w1, w2, w3, w5, w6, w7 = _wmaps(ii)
    nc = build_fused(_fused_specs())
    base = {}
    for prefix, wm in (('s1_', w1), ('s2_', w2), ('s3_', w3), ('s5_', w5),
                       ('s6_', w6), ('s7_', w7)):
        for k, v in wm.items():
            if k == 'eye':
                base['eye'] = v
            else:
                base[prefix + k] = np.ascontiguousarray(v)
    in_maps = [dict(x=s, **base) for s in shards]
    res = run_bass_kernel_spmd(nc, in_maps, core_ids=list(range(8)))
    out = np.concatenate([np.asarray(r['y']) for r in res.results], axis=0)
    return out.reshape(16, T, D)
